# version 06
# baseline (speedup 1.0000x reference)
import numpy as np
import ml_dtypes
from contextlib import ExitStack

import concourse.bass as bass
import concourse.mybir as mybir
from concourse.bass_utils import run_bass_kernel_spmd
from concourse.aot_env import allow_ici_all_to_all

F32 = mybir.dt.float32
BF16 = mybir.dt.bfloat16
I32 = mybir.dt.int32
AF = mybir.ActivationFunctionType
OP = mybir.AluOpType
AX = mybir.AxisListType

NC = 8
BB = 2
SEQ = 4096
DM = 4096
TOKS = BB * SEQ
DSL = DM // NC
NCH = DSL // 128
NST = 16
RK = 128
RXP = RK + 2 * NST
LS = 1024
NLI = SEQ // LS
II = 11008
ICH = II // 128
IHALF = ICH // 2
TPH = 512
EPS = 1e-5
STOP = "full"
SKIP_AG = False
SKIP_XP = False
SKIP_P0SC = False
SKIP_P0S = False
SKIP_P0V = False
DEBUG = False


def build_program():
    nc = bass.Bass("TRN2", target_bir_lowering=False)
    ap = lambda t: t.ap()

    x_tok = ap(nc.dram_tensor("x_tok", [1024, DM], F32, kind="ExternalInput"))
    x_dsl = ap(nc.dram_tensor("x_dsl", [TOKS, DSL], F32, kind="ExternalInput"))
    x_ffn = ap(nc.dram_tensor("x_ffn", [1024, DM], F32, kind="ExternalInput"))
    xpw_h = ap(nc.dram_tensor("xpw_h", [128, 32 * RXP], F32, kind="ExternalInput"))
    dtw_c = ap(nc.dram_tensor("dtw_c", [128, DSL], F32, kind="ExternalInput"))
    A_c = ap(nc.dram_tensor("A_c", [128, NCH * NST], F32, kind="ExternalInput"))
    D_c = ap(nc.dram_tensor("D_c", [128, NCH], F32, kind="ExternalInput"))
    dtb_c = ap(nc.dram_tensor("dtb_c", [128, NCH], F32, kind="ExternalInput"))
    sel_in = ap(nc.dram_tensor("sel_in", [32, 32 * 128], BF16, kind="ExternalInput"))
    ident_in = ap(nc.dram_tensor("ident_in", [128, 128], F32, kind="ExternalInput"))
    colsum_in = ap(nc.dram_tensor("colsum_in", [128, 128], F32, kind="ExternalInput"))
    ones0_in = ap(nc.dram_tensor("ones0_in", [128, 128], F32, kind="ExternalInput"))
    gs_in = ap(nc.dram_tensor("gs_in", [128, 1], F32, kind="ExternalInput"))
    us_in = ap(nc.dram_tensor("us_in", [128, 1], F32, kind="ExternalInput"))
    ds_in = ap(nc.dram_tensor("ds_in", [128, 1], F32, kind="ExternalInput"))
    consts_in = ap(nc.dram_tensor("consts_in", [128, 4], F32, kind="ExternalInput"))
    if STOP != "full":
        gwT = uwT = dwT = None
    else:
        gwT = ap(nc.dram_tensor("gwT", [DM, II], I32, kind="ExternalInput"))
        uwT = ap(nc.dram_tensor("uwT", [DM, II], I32, kind="ExternalInput"))
        dwT = ap(nc.dram_tensor("dwT", [II, DM], I32, kind="ExternalInput"))
    out_c = ap(nc.dram_tensor("out_c", [1024, DM], F32, kind="ExternalOutput"))
    if DEBUG:
        dbg_xp = ap(nc.dram_tensor("dbg_xp", [NC * RXP, 1024], F32, kind="ExternalOutput"))
        dbg_dt0 = ap(nc.dram_tensor("dbg_dt0", [128, NCH * LS], BF16, kind="ExternalOutput"))
        dbg_yA = ap(nc.dram_tensor("dbg_yA", [NC * TPH, DSL], F32, kind="ExternalOutput"))
        dbg_x1T = ap(nc.dram_tensor("dbg_x1T", [128, 32 * TPH], BF16, kind="ExternalOutput"))
        dbg_a2ao = ap(nc.dram_tensor("dbg_a2ao", [NC * TPH, DSL], F32, kind="ExternalOutput"))
        dbg_yslab = ap(nc.dram_tensor("dbg_yslab", [128, 16], F32, kind="ExternalOutput"))
        dbg_safe = ap(nc.dram_tensor("dbg_safe", [128, DSL], F32, kind="ExternalOutput"))
        dbg_lny_gp = ap(nc.dram_tensor("dbg_lny_gp", [128, 2048], F32, kind="ExternalOutput"))
        dbg_ln = ap(nc.dram_tensor("dbg_ln", [128, 8], F32, kind="ExternalOutput"))
        dbg_zh0 = ap(nc.dram_tensor("dbg_zh0", [128, 2048], F32, kind="ExternalOutput"))
        dbg_ty = ap(nc.dram_tensor("dbg_ty", [128, 2048], F32, kind="ExternalOutput"))

    ag_in = ap(nc.dram_tensor("ag_in", [RXP, 1024], F32))
    ag_out = ap(nc.dram_tensor("ag_out", [NC * RXP, 1024], F32))
    a2aA_in = ap(nc.dram_tensor("a2aA_in", [NC * TPH, DSL], F32))
    a2aA_out = ap(nc.dram_tensor("a2aA_out", [NC * TPH, DSL], F32))
    a2aB_in = ap(nc.dram_tensor("a2aB_in", [NC * TPH, DSL], F32))
    a2aB_out = ap(nc.dram_tensor("a2aB_out", [NC * TPH, DSL], F32))
    part_dram = ap(nc.dram_tensor("part_dram", [DM, TPH], F32))
    a2aA_safe = ap(nc.dram_tensor("a2aA_safe", [NC * TPH, DSL], F32))
    a2aB_safe = ap(nc.dram_tensor("a2aB_safe", [NC * TPH, DSL], F32))
    flag_in = ap(nc.dram_tensor("flag_in", [1, 4], F32))
    flagA_out = ap(nc.dram_tensor("flagA_out", [1, 4], F32))
    flagB_out = ap(nc.dram_tensor("flagB_out", [1, 4], F32))
    flagG_out = ap(nc.dram_tensor("flagG_out", [1, 4], F32))

    root = ExitStack()
    er = root.enter_context

    sem = {}
    for s in ["dS", "dG", "sT", "sV", "sS", "sG", "cAG", "cA", "cB",
              "fAG", "fA", "fB", "dY"]:
        sem[s] = er(nc.semaphore(s))
    cnt = {s: 0 for s in sem}

    R = dict(side="right")
    ident_sb = er(nc.sbuf_tensor("ident_sb", [128, 128], F32, **R))
    dbgslab_sb = er(nc.sbuf_tensor("dbgslab_sb", [128, 16], F32, **R))
    colsum_sb = er(nc.sbuf_tensor("colsum_sb", [128, 128], F32, **R))
    ones0_sb = er(nc.sbuf_tensor("ones0_sb", [128, 128], F32, **R))
    gs_sb = er(nc.sbuf_tensor("gs_sb", [128, 1], F32, **R))
    us_sb = er(nc.sbuf_tensor("us_sb", [128, 1], F32, **R))
    ds_sb = er(nc.sbuf_tensor("ds_sb", [128, 1], F32, **R))
    const_sb = er(nc.sbuf_tensor("const_sb", [128, 4], F32, **R))
    x1T = er(nc.sbuf_tensor("x1T", [128, 32 * TPH], BF16, **R))
    midreg = ExitStack()
    emid = midreg.enter_context
    A_sb = emid(nc.sbuf_tensor("A_sb", [128, NCH * NST], F32, **R))
    An_sb = emid(nc.sbuf_tensor("An_sb", [128, NCH * NST], F32, **R))
    D_sb = emid(nc.sbuf_tensor("D_sb", [128, NCH], F32, **R))
    dtb_sb = emid(nc.sbuf_tensor("dtb_sb", [128, NCH], F32, **R))
    hstate = emid(nc.sbuf_tensor("hstate", [128, BB * NCH * NST], F32, **R))
    sel_sb = emid(nc.sbuf_tensor("sel_sb", [32, 32 * 128], BF16, **R))
    dtw_sb = emid(nc.sbuf_tensor("dtw_sb", [128, DSL], BF16, **R))
    dtinT = emid(nc.sbuf_tensor("dtinT", [128, TOKS], BF16, **R))
    bcT = emid(nc.sbuf_tensor("bcT", [32, TOKS], BF16, **R))

    pf = [er(nc.psum_tensor(f"pf{i}", [128, 512], F32)) for i in range(4)]
    pf4 = er(nc.psum_tensor("pf4", [128, 512], F32))
    pstat = er(nc.psum_tensor("pstat", [128, 512], F32))
    pb = er(nc.psum_tensor("pb", [128, 1024], F32))

    bank_lastV = [0] * 7
    bank_lastS = [0] * 7

    ops = {k: [] for k in ["sync", "tensor", "vector", "scalar", "gpsimd"]}

    def em(eng, fn):
        ops[eng].append(fn)

    def inc(s, n=1):
        cnt[s] += n
        return cnt[s]

    def bsnap(*bids):
        return [(bank_lastV[b], bank_lastS[b]) for b in bids]

    def t_wait_snap(tensor, snap):
        for v_, s_ in snap:
            if v_ > 0:
                tensor.wait_ge(sem["sV"], v_)
            if s_ > 0:
                tensor.wait_ge(sem["sS"], s_)

    def _dbg_finish(wait_emits):
        for eng, fn in wait_emits:
            em(eng, fn)
        target = inc("dS", 16)
        def dbg_store(sync, target=target):
            sync.dma_start(out=out_c[0:128, 0:4], in_=const_sb[:, :]).then_inc(sem["dS"], 16)
            sync.wait_ge(sem["dS"], target)
        em("sync", dbg_store)
        blk = er(nc.Block())
        def play(name):
            def runner(eng):
                for fn in ops[name]:
                    fn(eng)
            return runner
        blk.sync(play("sync"))
        blk.tensor(play("tensor"))
        blk.vector(play("vector"))
        blk.scalar(play("scalar"))
        blk.gpsimd(play("gpsimd"))
        try:
            midreg.close()
        except Exception:
            pass
        root.close()
        return nc

    def p0_sync(sync):
        sync.dma_start(out=sel_sb[:, :], in_=sel_in).then_inc(sem["dS"], 16)
        sync.dma_start(out=ident_sb[:, :], in_=ident_in).then_inc(sem["dS"], 16)
        sync.dma_start(out=colsum_sb[:, :], in_=colsum_in).then_inc(sem["dS"], 16)
        sync.dma_start(out=ones0_sb[:, :], in_=ones0_in).then_inc(sem["dS"], 16)
        sync.dma_start(out=A_sb[:, :], in_=A_c).then_inc(sem["dS"], 16)
        sync.dma_start(out=D_sb[:, :], in_=D_c).then_inc(sem["dS"], 16)
        sync.dma_start(out=dtb_sb[:, :], in_=dtb_c).then_inc(sem["dS"], 16)
        sync.dma_start(out=gs_sb[:, :], in_=gs_in).then_inc(sem["dS"], 16)
        sync.dma_start(out=us_sb[:, :], in_=us_in).then_inc(sem["dS"], 16)
        sync.dma_start(out=ds_sb[:, :], in_=ds_in).then_inc(sem["dS"], 16)
        sync.dma_start(out=const_sb[:, :], in_=consts_in).then_inc(sem["dS"], 16)
    em("sync", p0_sync)
    d_small = inc("dS", 176)

    if STOP == "min1":
        d = cnt["dS"]
        return _dbg_finish([("sync", lambda s, d=d: s.wait_ge(sem["dS"], d))])

    p0reg = ExitStack()
    e0 = p0reg.enter_context
    xin = [e0(nc.sbuf_tensor(f"xin{i}", [128, DM], F32)) for i in range(2)]
    xT = e0(nc.sbuf_tensor("xT", [128, 32 * 1024], BF16))
    xpT1 = e0(nc.sbuf_tensor("xpT1", [128, 1024], F32))
    xpT2 = e0(nc.sbuf_tensor("xpT2", [32, 1024], F32))
    xpw_sb = e0(nc.sbuf_tensor("xpw_sb", [128, 32 * RXP], BF16))

    def p0_gp(gp):
        gp.dma_start(out=dtw_sb[:, :], in_=dtw_c).then_inc(sem["dG"], 16)
        gp.dma_start(out=xpw_sb[:, :], in_=xpw_h).then_inc(sem["dG"], 16)
    em("gpsimd", p0_gp)
    d_casts = inc("dG", 32)

    if STOP == "min2":
        d = cnt["dS"]; dg = cnt["dG"]
        return _dbg_finish([("sync", lambda s, d=d, dg=dg: (s.wait_ge(sem["dS"], d),
                                                            s.wait_ge(sem["dG"], dg)))])

    if SKIP_P0SC:
        def p0_scalar(sc):
            sc.wait_ge(sem["dS"], d_small)
            sc.nop().then_inc(sem["sS"], 1)
        em("scalar", p0_scalar)
        sS_expA = inc("sS")
        def p0_vec(v):
            v.wait_ge(sem["sS"], sS_expA)
            v.nop().then_inc(sem["sV"], 1)
        em("vector", p0_vec)
        sV_init = inc("sV")
    else:
        def p0_scalar(sc):
            sc.wait_ge(sem["dS"], d_small)
            sc.wait_ge(sem["dG"], d_casts)
            sc.nop().then_inc(sem["sS"], 1)
        em("scalar", p0_scalar)
        sS_expA = inc("sS")
        def p0_vec(v):
            v.wait_ge(sem["sS"], sS_expA)
            v.nop().then_inc(sem["sV"], 1)
        em("vector", p0_vec)
        sV_init = inc("sV")

    if STOP == "load":
        dg = cnt["dG"]; sv = cnt["sV"]
        return _dbg_finish([("sync", lambda s, dg=dg, sv=sv: (s.wait_ge(sem["dG"], dg),
                                                             s.wait_ge(sem["sV"], sv)))])

    for t in range(0 if SKIP_XP else 8):
        def x_ld(sync, t=t):
            if t >= 2:
                sync.wait_ge(sem["sV"], sV_init + (t - 1) * 8)
            sync.dma_start(out=xin[t % 2][:, :],
                           in_=x_tok[t * 128:(t + 1) * 128, :]).then_inc(sem["dS"], 16)
        em("sync", x_ld)
        d_t = inc("dS", 16)
        for g in range(8):
            bid = g % 2
            def x_tp(tensor, t=t, g=g, bid=bid, d_t=d_t, snap=bsnap(bid)):
                tensor.wait_ge(sem["dS"], d_t)
                if t == 0 and g == 0:
                    tensor.wait_ge(sem["dS"], d_small)
                t_wait_snap(tensor, snap)
                for k in range(4):
                    dch = g * 4 + k
                    mm = tensor.transpose(pf[bid][:, k * 128:(k + 1) * 128],
                                          xin[t % 2][:, dch * 128:(dch + 1) * 128],
                                          ident_sb[:, :])
                    if k == 3:
                        mm.then_inc(sem["sT"], 1)
            em("tensor", x_tp)
            sT_g = inc("sT")
            def x_ev(v, t=t, g=g, bid=bid, sT_g=sT_g):
                v.wait_ge(sem["sT"], sT_g)
                for k in range(4):
                    dch = g * 4 + k
                    ins = v.tensor_copy(xT[:, dch * 1024 + t * 128: dch * 1024 + (t + 1) * 128],
                                        pf[bid][:, k * 128:(k + 1) * 128])
                    if k == 3:
                        ins.then_inc(sem["sV"], 1)
            em("vector", x_ev)
            bank_lastV[bid] = inc("sV")
    sV_xT = cnt["sV"]

    if STOP == "xt":
        sv = cnt["sV"]; dg = cnt["dG"]
        return _dbg_finish([("sync", lambda s, sv=sv, dg=dg: (s.wait_ge(sem["sV"], sv),
                                                              s.wait_ge(sem["dG"], dg)))])

    def xp_mm(tensor, snap=bsnap(0, 1, 2, 3)):
        tensor.wait_ge(sem["sV"], sV_xT)
        tensor.wait_ge(sem["dG"], d_casts)
        t_wait_snap(tensor, snap)
        for half in range(2):
            for po in range(2):
                dst = pf[half] if po == 0 else pf[2 + half]
                rows = 128 if po == 0 else 32
                for dch in range(32):
                    stat = xpw_sb[:, dch * RXP + po * 128: dch * RXP + po * 128 + rows]
                    mm = tensor.matmul(dst[0:rows, :], stat,
                                       xT[:, dch * 1024 + half * 512: dch * 1024 + half * 512 + 512],
                                       start=(dch == 0), stop=(dch == 31))
                    if dch == 31:
                        mm.then_inc(sem["sT"], 1)
    if SKIP_XP:
        def xp_skip(tensor):
            tensor.wait_ge(sem["sV"], sV_xT)
            tensor.nop().then_inc(sem["sT"], 4)
        em("tensor", xp_skip)
    else:
        em("tensor", xp_mm)
    sT_xp = inc("sT", 4)

    def xp_ev(v):
        v.wait_ge(sem["sT"], sT_xp)
        if SKIP_XP:
            v.nop().then_inc(sem["sV"], 1)
            return
        v.tensor_copy(xpT1[:, 0:512], pf[0][:, :])
        v.tensor_copy(xpT1[:, 512:1024], pf[1][:, :])
        v.tensor_copy(xpT2[:, 0:512], pf[2][0:32, :])
        v.tensor_copy(xpT2[:, 512:1024], pf[3][0:32, :]).then_inc(sem["sV"], 1)
    em("vector", xp_ev)
    sV_xp = inc("sV")
    for bid in range(4):
        bank_lastV[bid] = sV_xp

    def xp_store(sync):
        sync.wait_ge(sem["sV"], sV_xp)
        sync.dma_start(out=ag_in[0:128, :], in_=xpT1[:, :]).then_inc(sem["dS"], 16)
        sync.dma_start(out=ag_in[128:160, :], in_=xpT2[:, :]).then_inc(sem["dS"], 16)
    em("sync", xp_store)
    d_ag_in = inc("dS", 32)

    if STOP == "xp":
        d = cnt["dS"]
        return _dbg_finish([("sync", lambda s, d=d: s.wait_ge(sem["dS"], d))])

    def ag_coll(gp):
        gp.wait_ge(sem["dS"], d_ag_in)
        if not SKIP_AG:
            gp.collective_compute("AllGather", OP.bypass,
                                  replica_groups=[list(range(NC))],
                                  ins=[ag_in.opt()], outs=[ag_out.opt()]).then_inc(sem["cAG"])
            gp.collective_compute("AllReduce", OP.add,
                                  replica_groups=[list(range(NC))],
                                  ins=[flag_in.opt()], outs=[flagG_out.opt()]).then_inc(sem["fAG"])
            gp.wait_ge(sem["cAG"], 1)
            gp.wait_ge(sem["fAG"], 1)
        src_t = ag_out if not SKIP_AG else ag_in
        for r in range(NC):
            s0 = (r * RXP) if not SKIP_AG else 0
            gp.dma_start(out=dtinT[:, r * 1024:(r + 1) * 1024],
                         in_=src_t[s0: s0 + 128, :]).then_inc(sem["dG"], 16)
            gp.dma_start(out=bcT[0:32, r * 1024:(r + 1) * 1024],
                         in_=src_t[s0 + 128: s0 + 160, :]).then_inc(sem["dG"], 16)
    em("gpsimd", ag_coll)
    if not SKIP_AG:
        inc("cAG")
        inc("fAG")
    d_dtin = inc("dG", 256)
    if DEBUG:
        def dump_xp(gp):
            gp.dma_start(out=dbg_xp, in_=ag_out).then_inc(sem["dG"], 16)
        em("gpsimd", dump_xp)
        inc("dG", 16)
        d_dtin = cnt["dG"]

    p0reg.close()

    if STOP == "p0":
        dgt = cnt["dG"]
        target = inc("dS", 16)
        def dbg_store(sync, dgt=dgt, target=target):
            sync.wait_ge(sem["dG"], dgt)
            sync.dma_start(out=out_c[0:128, 0:4], in_=const_sb[:, :]).then_inc(sem["dS"], 16)
            sync.wait_ge(sem["dS"], target)
        em("sync", dbg_store)
        blk = er(nc.Block())
        def play(name):
            def runner(eng):
                for fn in ops[name]:
                    fn(eng)
            return runner
        blk.sync(play("sync"))
        blk.tensor(play("tensor"))
        blk.vector(play("vector"))
        blk.scalar(play("scalar"))
        blk.gpsimd(play("gpsimd"))
        midreg.close()
        root.close()
        return nc

    screg = ExitStack()
    e1 = screg.enter_context
    dt_sb = e1(nc.sbuf_tensor("dt_sb", [128, NCH * LS], BF16))
    sp_r = e1(nc.sbuf_tensor("sp_r", [128, 512], F32))
    sp_a = e1(nc.sbuf_tensor("sp_a", [128, 512], F32))
    sp_e = e1(nc.sbuf_tensor("sp_e", [128, 512], F32))
    sp_l = [e1(nc.sbuf_tensor(f"sp_l{i}", [128, 512], F32)) for i in range(2)]
    xd_in = e1(nc.sbuf_tensor("xd_in", [128, 8 * DSL], F32))
    xTd = e1(nc.sbuf_tensor("xTd", [128, NCH * LS], BF16))
    dtx = e1(nc.sbuf_tensor("dtx", [128, NCH * LS], BF16))
    dA_sb = e1(nc.sbuf_tensor("dA_sb", [128, NCH * LS], BF16))
    dBx_sb = e1(nc.sbuf_tensor("dBx_sb", [128, NCH * LS], BF16))
    h_sb = e1(nc.sbuf_tensor("h_sb", [128, NCH * LS], F32))
    tmp_sb = e1(nc.sbuf_tensor("tmp_sb", [128, NCH * LS], F32))
    y_pp = [e1(nc.sbuf_tensor(f"ypp{i}", [128, NCH * LS], F32)) for i in range(2)]
    yT_sb = e1(nc.sbuf_tensor("yT_sb", [128, DSL], F32))

    dS_after_yA = None
    dS_after_yB = None
    sS_dA_last_prev = 0
    sT_ytp_last_prev = 0
    sG_acc_prev = 0
    sT_xd_free_prev = 0
    sV_dtsum_hist = []
    dbg_dt0_done = 0

    for li_b in range(NLI * BB):
        li, b = li_b // BB, li_b % BB
        l0 = li * LS
        g0 = b * SEQ + l0

        def xd_load(sync, g0=g0, w=sT_xd_free_prev, first=(li_b == 0)):
            if first:
                sync.wait_ge(sem["dG"], d_dtin)
            if w:
                sync.wait_ge(sem["sT"], w)
            for lt in range(8):
                sync.dma_start(out=xd_in[:, lt * DSL:(lt + 1) * DSL],
                               in_=x_dsl[g0 + lt * 128: g0 + (lt + 1) * 128, :]).then_inc(sem["dS"], 16)
        em("sync", xd_load)
        d_xd = inc("dS", 128)

        if STOP == "sxd" and li_b == 0:
            d = cnt["dS"]
            return _dbg_finish([("sync", lambda s, d=d: s.wait_ge(sem["dS"], d))])

        sV_dt_parts = []
        n_idx = NCH * 2
        dt_parts = "mvst"
        if STOP.startswith("sdt") and len(STOP) > 3 and li_b == 0:
            if STOP[3:].isdigit():
                n_idx = int(STOP[3:])
            else:
                n_idx = 1
                dt_parts = STOP[3:]
        for idx in range(n_idx):
            ch, hf = idx // 2, idx % 2
            bid = idx % 2
            par = idx % 2
            def dt_mm(tensor, ch=ch, hf=hf, bid=bid, g0=g0, li_b=li_b, idx=idx, snap=bsnap(bid)):
                if li_b == 0 and idx == 0:
                    tensor.wait_ge(sem["dG"], d_dtin)
                t_wait_snap(tensor, snap)
                tensor.matmul(pf[bid][:, :], dtw_sb[:, ch * 128:(ch + 1) * 128],
                              dtinT[:, g0 + hf * 512: g0 + (hf + 1) * 512],
                              start=True, stop=True).then_inc(sem["sT"], 1)
            em("tensor", dt_mm)
            sT_dt = inc("sT")

            if "v" not in dt_parts:
                def dt_skip_v(v, sT_dt=sT_dt):
                    v.wait_ge(sem["sT"], sT_dt)
                    v.nop().then_inc(sem["sV"], 2)
                em("vector", dt_skip_v)
                sV_r = inc("sV", 2)
                if not any(c in dt_parts for c in "aes"):
                    def dt_skip_s(sc, sT_dt=sT_dt):
                        sc.wait_ge(sem["sT"], sT_dt)
                        sc.nop().then_inc(sem["sS"], 1)
                    em("scalar", dt_skip_s)
                    sS_l = inc("sS")
                bank_lastS[bid] = cnt["sS"]
                bank_lastV[bid] = cnt["sV"]
                sV_dt_parts.append(cnt["sV"])
                sV_dtsum_hist.append(cnt["sV"])
                continue

            dbg_w = (dbg_dt0_done if (DEBUG and li_b == 1 and idx == 0) else 0)
            def dt_ep_v(v, ch=ch, bid=bid, sT_dt=sT_dt, war=sS_dA_last_prev, idx=idx, dbg_w=dbg_w):
                v.wait_ge(sem["sT"], sT_dt)
                if dbg_w:
                    v.wait_ge(sem["dS"], dbg_w)
                if idx == 0 and war:
                    v.wait_ge(sem["sS"], war)
                v.tensor_scalar(sp_r[:, :], pf[bid][:, :], dtb_sb[:, ch:ch + 1], 0.0,
                                OP.add, OP.max).then_inc(sem["sV"], 1)
            em("vector", dt_ep_v)
            sV_r = inc("sV")

            w_spl = sV_dtsum_hist[-2] if len(sV_dtsum_hist) >= 2 else 0
            def dt_ep_s(sc, ch=ch, bid=bid, par=par, sT_dt=sT_dt, w_spl=w_spl, parts=dt_parts):
                sc.wait_ge(sem["sT"], sT_dt)
                if w_spl:
                    sc.wait_ge(sem["sV"], w_spl)
                ins = sc.activation(sp_a[:, :], pf[bid][:, :], AF.Abs, bias=dtb_sb[:, ch:ch + 1])
                if "e" in parts or "s" in parts:
                    ins = sc.activation(sp_e[:, :], sp_a[:, :], AF.Exp, scale=const_sb[:, 2:3])
                if "s" in parts:
                    ins = sc.activation(sp_l[par][:, :], sp_e[:, :], AF.Ln,
                                        bias=const_sb[:, 0:1])
                ins.then_inc(sem["sS"], 1)
            if not any(c in dt_parts for c in "aes"):
                def dt_skip_s2(sc, sT_dt=sT_dt):
                    sc.wait_ge(sem["sT"], sT_dt)
                    sc.nop().then_inc(sem["sS"], 1)
                em("scalar", dt_skip_s2)
                sS_l = inc("sS")
                bank_lastS[bid] = sS_l
                bank_lastV[bid] = sV_r
                sV_dt_parts.append(cnt["sV"])
                sV_dtsum_hist.append(cnt["sV"])
                continue
            em("scalar", dt_ep_s)
            sS_l = inc("sS")
            bank_lastS[bid] = sS_l
            bank_lastV[bid] = sV_r

            if "t" not in dt_parts:
                sV_dt_parts.append(cnt["sV"])
                sV_dtsum_hist.append(cnt["sV"])
                continue

            def dt_sum(v, ch=ch, hf=hf, par=par, sS_l=sS_l):
                v.wait_ge(sem["sS"], sS_l)
                v.tensor_tensor(dt_sb[:, ch * LS + hf * 512: ch * LS + (hf + 1) * 512],
                                sp_r[:, :], sp_l[par][:, :], OP.add).then_inc(sem["sV"], 1)
            em("vector", dt_sum)
            sV_dt = inc("sV")
            sV_dtsum_hist.append(sV_dt)
            sV_dt_parts.append(sV_dt)
            if li_b == 0 and idx == 0:
                def a_prep_s(sc):
                    sc.activation(An_sb[:, :], A_sb[:, :], AF.Exp).then_inc(sem["sS"], 1)
                em("scalar", a_prep_s)
                sS_aprep = inc("sS")
                def a_prep_v(v, w=sS_aprep):
                    v.wait_ge(sem["sS"], w)
                    v.tensor_scalar(A_sb[:, :], An_sb[:, :], -1.0, None, OP.mult).then_inc(sem["sV"], 1)
                em("vector", a_prep_v)
                inc("sV")
        sV_dt_done = sV_dt_parts[-1]
        if DEBUG and li_b == 0:
            def dump_dt0(sync, w=sV_dt_done):
                sync.wait_ge(sem["sV"], w)
                sync.dma_start(out=dbg_dt0, in_=dt_sb[:, :]).then_inc(sem["dS"], 16)
            em("sync", dump_dt0)
            dbg_dt0_done = inc("dS", 16)

        if STOP.startswith("sdt") and li_b == 0:
            sv = cnt["sV"]; ss = cnt["sS"]
            return _dbg_finish([("sync", lambda s, sv=sv, ss=ss: (s.wait_ge(sem["sV"], sv),
                                                                 s.wait_ge(sem["sS"], ss)))])

        for lt in range(8):
            bid = 2 + lt % 2
            def x_tp2(tensor, lt=lt, bid=bid, d_xd=d_xd, snap=bsnap(bid)):
                tensor.wait_ge(sem["dS"], d_xd)
                t_wait_snap(tensor, snap)
                for ch in range(NCH):
                    mm = tensor.transpose(pf[bid][:, ch * 128:(ch + 1) * 128],
                                          xd_in[:, lt * DSL + ch * 128: lt * DSL + (ch + 1) * 128],
                                          ident_sb[:, :])
                    if ch == NCH - 1:
                        mm.then_inc(sem["sT"], 1)
            em("tensor", x_tp2)
            sT_x = inc("sT")
            def x_ev2(v, lt=lt, bid=bid, sT_x=sT_x):
                v.wait_ge(sem["sT"], sT_x)
                for ch in range(NCH):
                    ins = v.tensor_copy(xTd[:, ch * LS + lt * 128: ch * LS + (lt + 1) * 128],
                                        pf[bid][:, ch * 128:(ch + 1) * 128])
                    if ch == NCH - 1:
                        ins.then_inc(sem["sV"], 1)
            em("vector", x_ev2)
            bank_lastV[bid] = inc("sV")
        sT_xd_free_prev = cnt["sT"]

        def dtx_f(v, w_g=sG_acc_prev, w_t=sT_ytp_last_prev):
            if w_g:
                v.wait_ge(sem["sG"], w_g)
            if w_t:
                v.wait_ge(sem["sT"], w_t)
            for ch in range(NCH):
                v.tensor_tensor(dtx[:, ch * LS:(ch + 1) * LS],
                                dt_sb[:, ch * LS:(ch + 1) * LS],
                                xTd[:, ch * LS:(ch + 1) * LS], OP.mult)
            for ch in range(NCH):
                ins = v.tensor_scalar(y_pp[0][:, ch * LS:(ch + 1) * LS],
                                      xTd[:, ch * LS:(ch + 1) * LS],
                                      D_sb[:, ch:ch + 1], None, OP.mult)
                if ch == NCH - 1:
                    ins.then_inc(sem["sV"], 1)
        em("vector", dtx_f)
        sV_y0 = inc("sV")

        if STOP == "sx" and li_b == 0:
            sv = cnt["sV"]
            return _dbg_finish([("sync", lambda s, sv=sv: s.wait_ge(sem["sV"], sv))])

        for n in range(NST):
            war_dA = cnt["sV"] if n > 0 else 0
            def dA_f(sc, n=n, sV_dt_done=sV_dt_done, war=war_dA):
                if n == 0:
                    sc.wait_ge(sem["sV"], sV_dt_done)
                else:
                    sc.wait_ge(sem["sV"], war)
                for ch in range(NCH):
                    ins = sc.activation(dA_sb[:, ch * LS:(ch + 1) * LS],
                                        dt_sb[:, ch * LS:(ch + 1) * LS],
                                        AF.Exp, scale=A_sb[:, ch * NST + n: ch * NST + n + 1])
                    if ch == NCH - 1:
                        ins.then_inc(sem["sS"], 1)
            em("scalar", dA_f)
            sS_dA = inc("sS")

            prev_sG = cnt["sG"]
            for hl in range(2):
                def bc_mm(tensor, n=n, hl=hl, g0=g0, li_b=li_b, snap=bsnap(6)):
                    if li_b == 0 and n == 0 and hl == 0:
                        tensor.wait_ge(sem["dG"], d_dtin)
                    t_wait_snap(tensor, snap)
                    tensor.matmul(pb[:, 0:512],
                                  sel_sb[:, n * 128:(n + 1) * 128],
                                  bcT[:, g0 + hl * 512: g0 + (hl + 1) * 512],
                                  start=True, stop=True)
                    tensor.matmul(pb[:, 512:1024],
                                  sel_sb[:, (16 + n) * 128:(17 + n) * 128],
                                  bcT[:, g0 + hl * 512: g0 + (hl + 1) * 512],
                                  start=True, stop=True).then_inc(sem["sT"], 1)
                em("tensor", bc_mm)
                sT_bc = inc("sT")

                def v_nh(v, n=n, hl=hl, b=b, li=li, sT_bc=sT_bc, sS_dA=sS_dA, prev_sG=prev_sG):
                    v.wait_ge(sem["sT"], sT_bc)
                    if hl == 0:
                        v.wait_ge(sem["sS"], sS_dA)
                        if n > 0:
                            v.wait_ge(sem["sG"], prev_sG)
                    o = hl * 512
                    for ch in range(NCH):
                        v.tensor_tensor(dBx_sb[:, ch * LS + o: ch * LS + o + 512],
                                        dtx[:, ch * LS + o: ch * LS + o + 512],
                                        pb[:, 0:512], OP.mult)
                    for ch in range(NCH):
                        if hl == 0:
                            init = (0.0 if li == 0 else
                                    hstate[:, (b * NCH + ch) * NST + n:(b * NCH + ch) * NST + n + 1])
                        else:
                            init = h_sb[:, ch * LS + 511: ch * LS + 512]
                        v.tensor_tensor_scan(h_sb[:, ch * LS + o: ch * LS + o + 512],
                                             dA_sb[:, ch * LS + o: ch * LS + o + 512],
                                             dBx_sb[:, ch * LS + o: ch * LS + o + 512],
                                             init, OP.mult, OP.add)
                    for ch in range(NCH):
                        ins = v.tensor_tensor(tmp_sb[:, ch * LS + o: ch * LS + o + 512],
                                              h_sb[:, ch * LS + o: ch * LS + o + 512],
                                              pb[:, 512:1024], OP.mult)
                        if ch == NCH - 1 and hl == 0:
                            ins.then_inc(sem["sV"], 1)
                    if hl == 1:
                        for ch in range(NCH):
                            ins = v.tensor_copy(
                                hstate[:, (b * NCH + ch) * NST + n:(b * NCH + ch) * NST + n + 1],
                                h_sb[:, ch * LS + LS - 1: ch * LS + LS])
                            if ch == NCH - 1:
                                ins.then_inc(sem["sV"], 1)
                em("vector", v_nh)
                sV_n = inc("sV")
                bank_lastV[6] = sV_n

            def g_n(gp, n=n, sV_n=sV_n, sV_y0=sV_y0):
                gp.wait_ge(sem["sV"], sV_n)
                if n == 0:
                    gp.wait_ge(sem["sV"], sV_y0)
                for ch in range(NCH):
                    ins = gp.tensor_tensor(y_pp[(n + 1) % 2][:, ch * LS:(ch + 1) * LS],
                                           y_pp[n % 2][:, ch * LS:(ch + 1) * LS],
                                           tmp_sb[:, ch * LS:(ch + 1) * LS], OP.add)
                    if ch == NCH - 1:
                        ins.then_inc(sem["sG"], 1)
            em("gpsimd", g_n)
            inc("sG")
            if STOP == "sn1" and li_b == 0 and n == 0:
                sg = cnt["sG"]
                return _dbg_finish([("sync", lambda s, sg=sg: s.wait_ge(sem["sG"], sg))])
        if STOP == "sn" and li_b == 0:
            sg = cnt["sG"]
            return _dbg_finish([("sync", lambda s, sg=sg: s.wait_ge(sem["sG"], sg))])
        sS_dA_last_prev = cnt["sS"]
        sG_acc_prev = cnt["sG"]
        yfin = y_pp[0]

        for lt in range(8):
            bid = 2 + lt % 2
            def y_tp(tensor, lt=lt, bid=bid, sG_done=sG_acc_prev, snap=bsnap(bid)):
                tensor.wait_ge(sem["sG"], sG_done)
                t_wait_snap(tensor, snap)
                for ch in range(NCH):
                    mm = tensor.transpose(pf[bid][:, ch * 128:(ch + 1) * 128],
                                          yfin[:, ch * LS + lt * 128: ch * LS + (lt + 1) * 128],
                                          ident_sb[:, :])
                    if ch == NCH - 1:
                        mm.then_inc(sem["sT"], 1)
            em("tensor", y_tp)
            sT_y = inc("sT")
            d_prev_store = cnt["dS"]
            def y_ev(v, bid=bid, sT_y=sT_y, d_prev_store=d_prev_store):
                v.wait_ge(sem["sT"], sT_y)
                v.wait_ge(sem["dS"], d_prev_store)
                v.tensor_copy(yT_sb[:, :], pf[bid][:, :]).then_inc(sem["sV"], 1)
            em("vector", y_ev)
            sV_ye = inc("sV")
            bank_lastV[bid] = sV_ye
            l_abs = l0 + lt * 128
            phase = 0 if l_abs < 2048 else 1
            lph = l_abs - phase * 2048
            blk = b * 4 + lph // 512
            row = lph % 512
            dst = a2aA_in if phase == 0 else a2aB_in
            def y_st(sync, dst=dst, blk=blk, row=row, sV_ye=sV_ye):
                sync.wait_ge(sem["sV"], sV_ye)
                sync.dma_start(out=dst[blk * TPH + row: blk * TPH + row + 128, :],
                               in_=yT_sb[:, :]).then_inc(sem["dS"], 16)
            em("sync", y_st)
            inc("dS", 16)
        sT_ytp_last_prev = cnt["sT"]

        if STOP == "sy" and li_b == 0:
            d = cnt["dS"]
            return _dbg_finish([("sync", lambda s, d=d: s.wait_ge(sem["dS"], d))])
        if li_b == 3:
            dS_after_yA = cnt["dS"]
        if li_b == 7:
            dS_after_yB = cnt["dS"]

    if DEBUG:
        def dump_yA(gp, w=dS_after_yA):
            gp.wait_ge(sem["dS"], w)
            gp.dma_start(out=dbg_yA, in_=a2aA_in).then_inc(sem["dG"], 16)
        em("gpsimd", dump_yA)
        inc("dG", 16)

    def a2a_A(gp, w=dS_after_yA):
        gp.wait_ge(sem["dS"], w)
        gp.collective_compute("AllToAll", OP.bypass,
                              replica_groups=[list(range(NC))],
                              ins=[a2aA_in.opt()], outs=[a2aA_out.opt()]).then_inc(sem["cA"])
        gp.collective_compute("AllReduce", OP.add,
                              replica_groups=[list(range(NC))],
                              ins=[flag_in.opt()], outs=[flagA_out.opt()]).then_inc(sem["fA"])
        gp.wait_ge(sem["fA"], 1)
        gp.dma_start(out=a2aA_safe, in_=a2aA_out).then_inc(sem["dG"], 16)
        if DEBUG:
            gp.wait_ge(sem["fA"], 1)
            gp.dma_start(out=dbg_a2ao, in_=a2aA_out).then_inc(sem["dG"], 16)
            gp.dma_start(out=dbgslab_sb[:, :], in_=a2aA_out[0:128, 0:16]).then_inc(sem["dG"], 16)
            gp.wait_ge(sem["dG"], cnt_dG_dbg2)
            gp.dma_start(out=dbg_yslab, in_=dbgslab_sb[:, :]).then_inc(sem["dG"], 16)
    em("gpsimd", a2a_A)
    inc("cA")
    inc("fA")
    dG_a2aA_safe = inc("dG", 16)
    dbg_a2ao_done = 0
    if DEBUG:
        dbg_a2ao_done = inc("dG", 16)
        cnt_dG_dbg2 = inc("dG", 16)
        inc("dG", 16)

    def a2a_B(gp, w=dS_after_yB):
        gp.wait_ge(sem["dS"], w)
        gp.collective_compute("AllToAll", OP.bypass,
                              replica_groups=[list(range(NC))],
                              ins=[a2aB_in.opt()], outs=[a2aB_out.opt()]).then_inc(sem["cB"])
        gp.collective_compute("AllReduce", OP.add,
                              replica_groups=[list(range(NC))],
                              ins=[flag_in.opt()], outs=[flagB_out.opt()]).then_inc(sem["fB"])
        gp.wait_ge(sem["fB"], 1)
        gp.dma_start(out=a2aB_safe, in_=a2aB_out).then_inc(sem["dG"], 16)
    em("gpsimd", a2a_B)
    inc("cB")
    inc("fB")
    dG_a2aB_safe = inc("dG", 16)

    screg.close()
    midreg.close()

    if STOP == "scan":
        target = inc("dS", 16)
        def dbg_store2(sync, target=target):
            sync.wait_ge(sem["cB"], 1)
            sync.dma_start(out=out_c[0:128, 0:4], in_=const_sb[:, :]).then_inc(sem["dS"], 16)
            sync.wait_ge(sem["dS"], target)
        em("sync", dbg_store2)
        blk = er(nc.Block())
        def play(name):
            def runner(eng):
                for fn in ops[name]:
                    fn(eng)
            return runner
        blk.sync(play("sync"))
        blk.tensor(play("tensor"))
        blk.vector(play("vector"))
        blk.scalar(play("scalar"))
        blk.gpsimd(play("gpsimd"))
        root.close()
        return nc

    ffreg = ExitStack()
    e2 = ffreg.enter_context
    statspad = e2(nc.sbuf_tensor("statspad", [128, 512], F32))
    stats_m = e2(nc.sbuf_tensor("stats_m", [1, 512], F32))
    stats_va = e2(nc.sbuf_tensor("stats_va", [1, 512], F32))
    stats_t2 = e2(nc.sbuf_tensor("stats_t2", [1, 512], F32))
    stats_rs = e2(nc.sbuf_tensor("stats_rs", [1, 512], F32))
    ln_y = e2(nc.sbuf_tensor("ln_y", [128, 2048], F32))
    ln_x = e2(nc.sbuf_tensor("ln_x", [128, 2048], F32))
    ln_zh = [e2(nc.sbuf_tensor(f"ln_z{i}", [128, 2048], F32)) for i in range(2)]
    ln_s1 = e2(nc.sbuf_tensor("ln_s1", [128, 1], F32))
    ln_s1b = e2(nc.sbuf_tensor("ln_s1b", [128, 1], F32))
    ln_s2 = e2(nc.sbuf_tensor("ln_s2", [128, 1], F32))
    ln_s2b = e2(nc.sbuf_tensor("ln_s2b", [128, 1], F32))
    ln_m = e2(nc.sbuf_tensor("ln_m", [128, 1], F32))
    ln_v2 = e2(nc.sbuf_tensor("ln_v2", [128, 1], F32))
    ln_va = e2(nc.sbuf_tensor("ln_va", [128, 1], F32))
    ln_rs = e2(nc.sbuf_tensor("ln_rs", [128, 1], F32))
    gu_sb = e2(nc.sbuf_tensor("gu_sb", [128, IHALF * TPH], BF16))
    gw_sb = e2(nc.sbuf_tensor("gw_sb", [128, 32 * 128], BF16))
    uw_sb = e2(nc.sbuf_tensor("uw_sb", [128, 32 * 128], BF16))
    dwp_sb = e2(nc.sbuf_tensor("dwp_sb", [128, IHALF * 256], BF16))
    sg_sb = [e2(nc.sbuf_tensor(f"sg_sb{i}", [128, TPH], BF16)) for i in range(2)]
    zt_bf = e2(nc.sbuf_tensor("zt_bf", [128, 32 * TPH], BF16))
    dn_t1 = e2(nc.sbuf_tensor("dn_t1", [128, TPH], F32))
    dn_t2 = e2(nc.sbuf_tensor("dn_t2", [128, TPH], F32))
    dn_sq = e2(nc.sbuf_tensor("dn_sq", [128, TPH], F32))
    mrep_sb = e2(nc.sbuf_tensor("mrep_sb", [128, TPH], BF16))
    rsrep_sb = e2(nc.sbuf_tensor("rsrep_sb", [128, TPH], BF16))
    ostage = [e2(nc.sbuf_tensor(f"ostage{i}", [128, 512], F32)) for i in range(2)]
    stage_sb = [e2(nc.sbuf_tensor(f"stage{i}", [128, 512], F32)) for i in range(2)]

    sT_gu_last = 0
    sT_dn_last = 0
    sT_gu_hist = {}
    for phase in range(2):
        a2a_out_t = a2aA_safe if phase == 0 else a2aB_safe
        d_safe = dG_a2aA_safe if phase == 0 else dG_a2aB_safe
        csem = "cA" if phase == 0 else "cB"

        fsem = "fA" if phase == 0 else "fB"
        def wait_coll(sync, csem=fsem, phase=phase, w=dS_after_yB, wdbg=dbg_a2ao_done):
            sync.wait_ge(sem[csem], 1)
            if DEBUG and phase == 0 and wdbg:
                sync.wait_ge(sem["dG"], wdbg)
            if phase == 0:
                sync.wait_ge(sem["dS"], w)
        em("sync", wait_coll)

        sV_ln_prev = 0
        sS_sq_prev = 0
        for tt in range(4):
            d_zh = []
            for hh in range(2):
                wv_h = sV_ln_prev if hh == 0 else cnt["sV"]
                def y_load(gp, tt=tt, hh=hh, phase=phase, wv=wv_h, ws=sS_sq_prev,
                           fsem=("fA" if phase == 0 else "fB"), wsc=dS_after_yB,
                           wg=sG_acc_prev, wt=sT_ytp_last_prev, wsafe=d_safe,
                           a2a_out_t=a2a_out_t):
                    if tt == 0 and hh == 0:
                        gp.wait_ge(sem[fsem], 1)
                        gp.wait_ge(sem["dG"], wsafe)
                        if phase == 0:
                            gp.wait_ge(sem["dS"], wsc)
                            gp.wait_ge(sem["sG"], wg)
                            gp.wait_ge(sem["sT"], wt)
                    if wv:
                        gp.wait_ge(sem["sV"], wv)
                    if hh == 0 and ws:
                        gp.wait_ge(sem["sS"], ws)
                    for i in range(4 * hh, 4 * hh + 4):
                        gp.dma_start(out=ln_y[:, (i - 4 * hh) * DSL:(i - 4 * hh + 1) * DSL],
                                     in_=a2a_out_t[i * TPH + tt * 128: i * TPH + (tt + 1) * 128, :]
                                     ).then_inc(sem["dY"], 16)
                em("gpsimd", y_load)
                d_zg = inc("dY", 64)
                if DEBUG and phase == 0 and tt == 0 and hh == 0:
                    def dump_gp_side(gp, wg=d_zg):
                        gp.wait_ge(sem["dY"], wg)
                        gp.dma_start(out=dbg_safe, in_=a2aA_safe[0:128, :]).then_inc(sem["dG"], 16)
                        gp.dma_start(out=dbg_lny_gp, in_=ln_y[:, :]).then_inc(sem["dG"], 16)
                        gp.wait_ge(sem["dG"], cnt_gp_dump)
                    em("gpsimd", dump_gp_side)
                    cnt_gp_dump = inc("dG", 32)
                def z_load(sync, tt=tt, hh=hh, phase=phase, wv=wv_h, ws=sS_sq_prev):
                    if wv:
                        sync.wait_ge(sem["sV"], wv)
                    if hh == 0 and ws:
                        sync.wait_ge(sem["sS"], ws)
                    sync.dma_start(out=ln_x[:, :],
                                   in_=x_ffn[phase * TPH + tt * 128: phase * TPH + (tt + 1) * 128,
                                             hh * 2048:(hh + 1) * 2048]).then_inc(sem["dS"], 16)
                em("sync", z_load)
                d_z = inc("dS", 16)
                d_zh.append((d_z, d_zg))
                if DEBUG and phase == 0 and tt == 0 and hh == 0:
                    def dump_lny(sync, w=d_z, wg=d_zg):
                        sync.wait_ge(sem["dS"], w)
                        sync.wait_ge(sem["dY"], wg)
                        sync.dma_start(out=dbg_ty, in_=ln_y[:, :]).then_inc(sem["dS"], 16)
                    em("sync", dump_lny)
                    dbg_lny_done = inc("dS", 16)
                else:
                    dbg_lny_done = 0
                def z_add(v, hh=hh, d_z=d_z, d_zg=d_zg, wdbg=dbg_lny_done):
                    v.wait_ge(sem["dS"], max(d_z, wdbg))
                    v.wait_ge(sem["dY"], d_zg)
                    v.tensor_tensor(ln_zh[hh][:, :], ln_x[:, :], ln_y[:, :], OP.add)
                    dst = ln_s1 if hh == 0 else ln_s1b
                    v.tensor_reduce(dst[:, :], ln_zh[hh][:, :], AX.X, OP.add)
                    v.tensor_copy(dn_t1[:, :], ln_zh[hh][:, 0:512]).then_inc(sem["sV"], 1)
                em("vector", z_add)
                inc("sV")
            sV_z = cnt["sV"]

            def ln1_m(v):
                v.tensor_tensor(ln_v2[:, :], ln_s1[:, :], ln_s1b[:, :], OP.add)
                v.tensor_copy(dn_t1[:, :], ln_zh[0][:, 0:512])
                v.tensor_scalar(ln_m[:, :], ln_v2[:, :], 1.0 / DM, None, OP.mult)
                v.tensor_copy(dn_t2[:, :], ln_zh[0][:, 0:512])
                v.tensor_scalar(ln_y[:, :], ln_zh[0][:, :], ln_m[:, 0:1], None, OP.subtract)
                v.tensor_scalar(ln_x[:, :], ln_zh[1][:, :], ln_m[:, 0:1], None,
                                OP.subtract).then_inc(sem["sV"], 1)
            em("vector", ln1_m)
            sV_t = inc("sV")

            if DEBUG and phase == 0 and tt == 0:
                def dump_zh(sync, w=sV_t):
                    sync.wait_ge(sem["sV"], w)
                    sync.dma_start(out=dbg_zh0, in_=ln_zh[0][:, :]).then_inc(sem["dS"], 16)
                em("sync", dump_zh)
                dbg_zh_done = inc("dS", 16)
            else:
                dbg_zh_done = 0
            def ln1_sq(sc, sV_t=sV_t, w=dbg_zh_done):
                sc.wait_ge(sem["sV"], sV_t)
                if w:
                    sc.wait_ge(sem["dS"], w)
                sc.activation(ln_zh[0][:, :], ln_y[:, :], AF.Square, accum_out=ln_s2[:, :])
                sc.activation(ln_zh[1][:, :], ln_x[:, :], AF.Square, accum_out=ln_s2b[:, :])
                sc.activation(dn_sq[:, :], ln_y[:, 0:512], AF.Copy).then_inc(sem["sS"], 1)
            em("scalar", ln1_sq)
            sS_sq = inc("sS")

            def ln1_var(v, sS_sq=sS_sq):
                v.wait_ge(sem["sS"], sS_sq)
                v.tensor_tensor(ln_v2[:, :], ln_s2[:, :], ln_s2b[:, :], OP.add)
                v.tensor_copy(dn_t1[:, :], ln_zh[1][:, 0:512])
                v.tensor_scalar(ln_va[:, :], ln_v2[:, :], 1.0 / DM, None, OP.mult)
                v.tensor_copy(dn_t2[:, :], ln_zh[1][:, 0:512]).then_inc(sem["sV"], 1)
            em("vector", ln1_var)
            sV_var = inc("sV")

            def ln1_rs(sc, sV_var=sV_var):
                sc.wait_ge(sem["sV"], sV_var)
                sc.activation(ln_v2[:, :], ln_va[:, :], AF.Ln, bias=const_sb[:, 1:2])
                sc.activation(dn_sq[:, :], ln_zh[0][:, 0:512], AF.Copy)
                sc.activation(ln_rs[:, :], ln_v2[:, :], AF.Exp, scale=const_sb[:, 3:4])
                sc.activation(dn_sq[:, :], ln_zh[1][:, 0:512], AF.Copy).then_inc(sem["sS"], 1)
            em("scalar", ln1_rs)
            sS_rs = inc("sS")

            if DEBUG and phase == 0 and tt == 0:
                def dump_ln(v, w=sS_rs):
                    v.wait_ge(sem["sS"], w)
                    v.tensor_copy(statspad[:, 0:1], ln_s1[:, :])
                    v.tensor_copy(statspad[:, 1:2], ln_s1b[:, :])
                    v.tensor_copy(statspad[:, 2:3], ln_s2[:, :])
                    v.tensor_copy(statspad[:, 3:4], ln_s2b[:, :])
                    v.tensor_copy(statspad[:, 4:5], ln_m[:, :])
                    v.tensor_copy(statspad[:, 5:6], ln_va[:, :])
                    v.tensor_copy(statspad[:, 6:7], ln_rs[:, :]).then_inc(sem["sV"], 1)
                em("vector", dump_ln)
                w_dln = inc("sV")
                def dump_ln_dma(sync, w=w_dln):
                    sync.wait_ge(sem["sV"], w)
                    sync.dma_start(out=dbg_ln, in_=statspad[:, 0:8]).then_inc(sem["dS"], 16)
                em("sync", dump_ln_dma)
                inc("dS", 16)
            def ln1_x1(v, sS_rs=sS_rs):
                v.wait_ge(sem["sS"], sS_rs)
                v.tensor_scalar(ln_zh[0][:, :], ln_y[:, :], ln_rs[:, 0:1], None, OP.mult)
                v.tensor_scalar(ln_zh[1][:, :], ln_x[:, :], ln_rs[:, 0:1], None,
                                OP.mult).then_inc(sem["sV"], 1)
            em("vector", ln1_x1)
            sV_x1 = inc("sV")

            for g in range(8):
                bid = g % 2
                hh = g // 4
                def x1_tp(tensor, tt=tt, g=g, bid=bid, hh=hh, sV_x1=sV_x1, snap=bsnap(bid)):
                    tensor.wait_ge(sem["sV"], sV_x1)
                    t_wait_snap(tensor, snap)
                    for k in range(4):
                        dloc = (g % 4) * 4 + k
                        mm = tensor.transpose(pf[bid][:, k * 128:(k + 1) * 128],
                                              ln_zh[hh][:, dloc * 128:(dloc + 1) * 128],
                                              ident_sb[:, :])
                        if k == 3:
                            mm.then_inc(sem["sT"], 1)
                em("tensor", x1_tp)
                sT_x1 = inc("sT")
                def x1_ev(v, tt=tt, g=g, bid=bid, sT_x1=sT_x1):
                    v.wait_ge(sem["sT"], sT_x1)
                    for k in range(4):
                        dch = g * 4 + k
                        ins = v.tensor_copy(x1T[:, dch * TPH + tt * 128: dch * TPH + (tt + 1) * 128],
                                            pf[bid][:, k * 128:(k + 1) * 128])
                        if k == 3:
                            ins.then_inc(sem["sV"], 1)
                em("vector", x1_ev)
                bank_lastV[bid] = inc("sV")
            sV_ln_prev = cnt["sV"]
            sS_sq_prev = cnt["sS"]
        sV_x1T_done = cnt["sV"]
        if DEBUG and phase == 0:
            def dump_x1T(sync, w=sV_x1T_done):
                sync.wait_ge(sem["sV"], w)
                sync.dma_start(out=dbg_x1T, in_=x1T[:, :]).then_inc(sem["dS"], 16)
            em("sync", dump_x1T)
            inc("dS", 16)

        sT_stats_final = None
        for hf in range(2):
            sV_gu_hist = {}
            sT_gu_hist = {}
            for ic in range(IHALF):
                Ic = hf * IHALF + ic
                def w_load(gp, Ic=Ic, w=(sT_gu_hist.get(ic - 1) or sT_gu_last)):
                    if w:
                        gp.wait_ge(sem["sT"], w)
                    gp.dma_start(out=gw_sb[:, :].rearrange("p (dc i) -> p dc i", dc=32),
                                 in_=gwT[:, Ic * 128:(Ic + 1) * 128].rearrange(
                                     "(dc p) i -> p dc i", p=128)).then_inc(sem["dG"], 16)
                    gp.dma_start(out=uw_sb[:, :].rearrange("p (dc i) -> p dc i", dc=32),
                                 in_=uwT[:, Ic * 128:(Ic + 1) * 128].rearrange(
                                     "(dc p) i -> p dc i", p=128)).then_inc(sem["dG"], 16)
                em("gpsimd", w_load)
                d_w = inc("dG", 32)

                def gu_mm(tensor, ic=ic, hf=hf, d_w=d_w, sV_x1T_done=sV_x1T_done,
                          snap=bsnap(4, 5)):
                    tensor.wait_ge(sem["dG"], d_w)
                    if ic == 0 and hf == 0:
                        tensor.wait_ge(sem["sV"], sV_x1T_done)
                    t_wait_snap(tensor, snap)
                    for dch in range(32):
                        tensor.matmul(pf4[:, :], gw_sb[:, dch * 128:(dch + 1) * 128],
                                      x1T[:, dch * TPH: dch * TPH + TPH],
                                      start=(dch == 0), stop=(dch == 31),
                                      skip_group_check=True)
                        mm = tensor.matmul(pstat[:, :], uw_sb[:, dch * 128:(dch + 1) * 128],
                                           x1T[:, dch * TPH: dch * TPH + TPH],
                                           start=(dch == 0), stop=(dch == 31),
                                           skip_group_check=True)
                        if dch == 31:
                            mm.then_inc(sem["sT"], 1)
                em("tensor", gu_mm)
                sT_gu = inc("sT")
                sT_gu_hist[ic] = sT_gu

                par = ic % 2
                w_sg = sV_gu_hist.get(ic - 2, 0)
                def silu_f(sc, sT_gu=sT_gu, par=par, w_sg=w_sg):
                    sc.wait_ge(sem["sT"], sT_gu)
                    if w_sg:
                        sc.wait_ge(sem["sV"], w_sg)
                    sc.activation(sg_sb[par][:, :], pf4[:, :], AF.Silu,
                                  scale=gs_sb[:, 0:1]).then_inc(sem["sS"], 1)
                em("scalar", silu_f)
                sS_si = inc("sS")
                bank_lastS[4] = sS_si

                def gu_ev(v, ic=ic, par=par, sS_si=sS_si):
                    v.wait_ge(sem["sS"], sS_si)
                    v.scalar_tensor_tensor(gu_sb[:, ic * TPH:(ic + 1) * TPH],
                                           pstat[:, :], us_sb[:, 0:1], sg_sb[par][:, :],
                                           OP.mult, OP.mult).then_inc(sem["sV"], 1)
                em("vector", gu_ev)
                sV_gu = inc("sV")
                sV_gu_hist[ic] = sV_gu
                bank_lastV[5] = sV_gu
            sV_gu_done = cnt["sV"]
            sT_gu_last = cnt["sT"]

            sT_dn_hist = {}
            sT_st_hist = {}
            dS_sp_hist = {}
            sV_rl_hist = {}
            if hf == 1:
                dS_spill_done = dS_spill_done_mark
            for pr in range(16):
                def dw_load(gp, pr=pr, hf=hf, w=(sT_dn_hist.get(pr - 1) or sT_dn_last)):
                    if w:
                        gp.wait_ge(sem["sT"], w)
                    gp.dma_start(out=dwp_sb[:, :].rearrange("p (ic d) -> p ic d", ic=IHALF),
                                 in_=dwT[hf * IHALF * 128:(hf + 1) * IHALF * 128,
                                         pr * 256:(pr + 1) * 256].rearrange(
                                     "(ic p) d -> p ic d", p=128)).then_inc(sem["dG"], 16)
                em("gpsimd", dw_load)
                d_dw = inc("dG", 16)

                def dn_mm(tensor, pr=pr, d_dw=d_dw, sV_gu_done=sV_gu_done, snap=bsnap(2, 3)):
                    tensor.wait_ge(sem["dG"], d_dw)
                    tensor.wait_ge(sem["sV"], sV_gu_done)
                    t_wait_snap(tensor, snap)
                    for ic in range(IHALF):
                        tensor.matmul(pf[2][:, :], dwp_sb[:, ic * 256: ic * 256 + 128],
                                      gu_sb[:, ic * TPH:(ic + 1) * TPH],
                                      start=(ic == 0), stop=(ic == IHALF - 1),
                                      skip_group_check=True)
                        mm = tensor.matmul(pf[3][:, :], dwp_sb[:, ic * 256 + 128:(ic + 1) * 256],
                                           gu_sb[:, ic * TPH:(ic + 1) * TPH],
                                           start=(ic == 0), stop=(ic == IHALF - 1),
                                           skip_group_check=True)
                        if ic == IHALF - 1:
                            mm.then_inc(sem["sT"], 1)
                em("tensor", dn_mm)
                sT_dn = inc("sT")
                sT_dn_hist[pr] = sT_dn
                sT_dn_last = sT_dn

                if hf == 0:
                    def dn_ev0(v, sT_dn=sT_dn, w=dS_sp_hist.get(pr - 1)):
                        v.wait_ge(sem["sT"], sT_dn)
                        if w:
                            v.wait_ge(sem["dS"], w)
                        v.tensor_copy(stage_sb[0][:, :], pf[2][:, :])
                        v.tensor_copy(stage_sb[1][:, :], pf[3][:, :]).then_inc(sem["sV"], 1)
                    em("vector", dn_ev0)
                    sV_st = inc("sV")
                    bank_lastV[2] = sV_st
                    bank_lastV[3] = sV_st
                    def dn_spill(sync, pr=pr, sV_st=sV_st):
                        sync.wait_ge(sem["sV"], sV_st)
                        sync.dma_start(out=part_dram[pr * 256: pr * 256 + 128, :],
                                       in_=stage_sb[0][:, :]).then_inc(sem["dS"], 16)
                        sync.dma_start(out=part_dram[pr * 256 + 128:(pr + 1) * 256, :],
                                       in_=stage_sb[1][:, :]).then_inc(sem["dS"], 16)
                    em("sync", dn_spill)
                    dS_sp_hist[pr] = inc("dS", 32)
                    dS_spill_done_mark = dS_sp_hist[pr]
                else:
                    def dn_reload(sync, pr=pr, w=sV_rl_hist.get(pr - 1), wsp=dS_spill_done):
                        sync.wait_ge(sem["dS"], wsp)
                        if w:
                            sync.wait_ge(sem["sV"], w)
                        sync.dma_start(out=stage_sb[0][:, :],
                                       in_=part_dram[pr * 256: pr * 256 + 128, :]).then_inc(sem["dS"], 16)
                        sync.dma_start(out=stage_sb[1][:, :],
                                       in_=part_dram[pr * 256 + 128:(pr + 1) * 256, :]).then_inc(sem["dS"], 16)
                    em("sync", dn_reload)
                    d_rl = inc("dS", 32)
                    for k in range(2):
                        dch = pr * 2 + k
                        w_st = sT_st_hist.get(dch - 1, 0)
                        def dn_ev1(v, k=k, dch=dch, sT_dn=sT_dn, d_rl=d_rl, w_st=w_st):
                            v.wait_ge(sem["sT"], sT_dn)
                            v.wait_ge(sem["dS"], d_rl)
                            if w_st:
                                v.wait_ge(sem["sT"], w_st)
                            v.tensor_tensor(dn_t1[:, :], pf[2 + k][:, :], stage_sb[k][:, :], OP.add)
                            v.scalar_tensor_tensor(dn_t2[:, :], dn_t1[:, :], ds_sb[:, 0:1],
                                                   x1T[:, dch * TPH:(dch + 1) * TPH],
                                                   OP.mult, OP.add)
                            v.tensor_copy(zt_bf[:, dch * TPH:(dch + 1) * TPH],
                                          dn_t2[:, :]).then_inc(sem["sV"], 1)
                        em("vector", dn_ev1)
                        sV_e1 = inc("sV")
                        bank_lastV[2 + k] = sV_e1
                        def dn_sq_f(sc, sV_e1=sV_e1, w_st=w_st):
                            sc.wait_ge(sem["sV"], sV_e1)
                            if w_st:
                                sc.wait_ge(sem["sT"], w_st)
                            sc.activation(dn_sq[:, :], dn_t2[:, :], AF.Square).then_inc(sem["sS"], 1)
                        em("scalar", dn_sq_f)
                        sS_q = inc("sS")
                        def st_mm(tensor, dch=dch, sV_e1=sV_e1, sS_q=sS_q, snap=bsnap(5)):
                            tensor.wait_ge(sem["sV"], sV_e1)
                            tensor.wait_ge(sem["sS"], sS_q)
                            if dch == 0:
                                t_wait_snap(tensor, snap)
                            tensor.matmul(pstat[0:1, :], colsum_sb[:, 0:1], dn_t2[:, :],
                                          start=(dch == 0), stop=(dch == 31),
                                          skip_group_check=True)
                            mm = tensor.matmul(pstat[32:33, :], colsum_sb[:, 0:1], dn_sq[:, :],
                                               start=False, stop=(dch == 31),
                                               skip_group_check=True)
                            mm.then_inc(sem["sT"], 1)
                        em("tensor", st_mm)
                        sT_st = inc("sT")
                        sT_st_hist[dch] = sT_st
                        if dch == 31:
                            sT_stats_final = sT_st
                    sV_rl_hist[pr] = cnt["sV"]

        def ln2_v(v, w=sT_stats_final, ph=phase):
            v.wait_ge(sem["sT"], w)
            if ph == 0:
                v.memset(statspad[:, :], 0.0)
            v.tensor_scalar(stats_m[:, :], pstat[0:1, :], 1.0 / DM, None, OP.mult)
            v.tensor_scalar(stats_va[:, :], pstat[32:33, :], 1.0 / DM, None, OP.mult)
            v.tensor_tensor(stats_t2[:, :], stats_m[:, :], stats_m[:, :], OP.mult)
            v.tensor_tensor(stats_rs[:, :], stats_va[:, :], stats_t2[:, :], OP.subtract)
            v.tensor_copy(statspad[0:1, 0:512], stats_m[:, :]).then_inc(sem["sV"], 1)
        em("vector", ln2_v)
        sV_st2 = inc("sV")
        bank_lastV[5] = sV_st2

        def ln2_rs(sc, sV_st2=sV_st2):
            sc.wait_ge(sem["sV"], sV_st2)
            sc.activation(stats_t2[:, :], stats_rs[:, :], AF.Ln, bias=const_sb[0:1, 1:2])
            sc.activation(stats_va[:, :], stats_t2[:, :], AF.Exp,
                          scale=const_sb[0:1, 3:4]).then_inc(sem["sS"], 1)
        em("scalar", ln2_rs)
        sS_rs2 = inc("sS")

        def bc_m(tensor, sV_st2=sV_st2, snap=bsnap(1)):
            tensor.wait_ge(sem["sV"], sV_st2)
            t_wait_snap(tensor, snap)
            tensor.matmul(pf[1][:, :], ones0_sb[:, :], statspad[:, :],
                          start=True, stop=True).then_inc(sem["sT"], 1)
        em("tensor", bc_m)
        sT_bm = inc("sT")

        def mrep_ev(v, sT_bm=sT_bm, sS_rs2=sS_rs2):
            v.wait_ge(sem["sT"], sT_bm)
            v.tensor_copy(mrep_sb[:, :], pf[1][:, :])
            v.wait_ge(sem["sS"], sS_rs2)
            v.tensor_copy(statspad[0:1, 0:512], stats_va[:, :]).then_inc(sem["sV"], 1)
        em("vector", mrep_ev)
        sV_mrep = inc("sV")
        bank_lastV[1] = sV_mrep

        def bc_rs(tensor, sV_mrep=sV_mrep):
            tensor.wait_ge(sem["sV"], sV_mrep)
            tensor.matmul(pf[1][:, :], ones0_sb[:, :], statspad[:, :],
                          start=True, stop=True).then_inc(sem["sT"], 1)
        em("tensor", bc_rs)
        sT_brs = inc("sT")

        def rsrep_ev(v, sT_brs=sT_brs):
            v.wait_ge(sem["sT"], sT_brs)
            v.tensor_copy(rsrep_sb[:, :], pf[1][:, :]).then_inc(sem["sV"], 1)
        em("vector", rsrep_ev)
        sV_rsrep = inc("sV")
        bank_lastV[1] = sV_rsrep

        dS_ost_hist = {}
        for tt in range(4):
            for g in range(8):
                bid = 2 + g % 2
                opar = g % 2
                w_ost = dS_ost_hist.get((tt, g - 2), 0)
                def zn_v(v, tt=tt, g=g, sV_rsrep=sV_rsrep):
                    v.wait_ge(sem["sV"], sV_rsrep)
                    for k in range(4):
                        dch = g * 4 + k
                        sl = slice(dch * TPH + tt * 128, dch * TPH + (tt + 1) * 128)
                        v.tensor_tensor(dn_t1[:, k * 128:(k + 1) * 128],
                                        zt_bf[:, sl], mrep_sb[:, tt * 128:(tt + 1) * 128],
                                        OP.subtract)
                    for k in range(4):
                        ins = v.tensor_tensor(dn_t2[:, k * 128:(k + 1) * 128],
                                              dn_t1[:, k * 128:(k + 1) * 128],
                                              rsrep_sb[:, tt * 128:(tt + 1) * 128], OP.mult)
                        if k == 3:
                            ins.then_inc(sem["sV"], 1)
                em("vector", zn_v)
                sV_zn = inc("sV")
                def zn_tp(tensor, g=g, bid=bid, sV_zn=sV_zn, snap=bsnap(bid)):
                    tensor.wait_ge(sem["sV"], sV_zn)
                    t_wait_snap(tensor, snap)
                    for k in range(4):
                        mm = tensor.transpose(pf[bid][:, k * 128:(k + 1) * 128],
                                              dn_t2[:, k * 128:(k + 1) * 128],
                                              ident_sb[:, :])
                        if k == 3:
                            mm.then_inc(sem["sT"], 1)
                em("tensor", zn_tp)
                sT_zn = inc("sT")
                def zn_ev(v, g=g, bid=bid, opar=opar, sT_zn=sT_zn, w_ost=w_ost):
                    v.wait_ge(sem["sT"], sT_zn)
                    if w_ost:
                        v.wait_ge(sem["dS"], w_ost)
                    for k in range(4):
                        ins = v.tensor_copy(ostage[opar][:, k * 128:(k + 1) * 128],
                                            pf[bid][:, k * 128:(k + 1) * 128])
                        if k == 3:
                            ins.then_inc(sem["sV"], 1)
                em("vector", zn_ev)
                sV_ze = inc("sV")
                bank_lastV[bid] = sV_ze
                def o_store(sync, tt=tt, g=g, opar=opar, phase=phase, sV_ze=sV_ze):
                    sync.wait_ge(sem["sV"], sV_ze)
                    sync.dma_start(out=out_c[phase * TPH + tt * 128: phase * TPH + (tt + 1) * 128,
                                             g * 512:(g + 1) * 512],
                                   in_=ostage[opar][:, :]).then_inc(sem["dS"], 16)
                em("sync", o_store)
                dS_ost_hist[(tt, g)] = inc("dS", 16)
            dS_ost_hist[(tt + 1, -2)] = dS_ost_hist[(tt, 6)]
            dS_ost_hist[(tt + 1, -1)] = dS_ost_hist[(tt, 7)]

    d_final = cnt["dS"]
    def fin_sync(sync):
        sync.wait_ge(sem["dS"], d_final)
    em("sync", fin_sync)
    dG_final = cnt["dG"]
    def fin_gp(gp):
        gp.wait_ge(sem["dG"], dG_final)
    em("gpsimd", fin_gp)

    ffreg.close()

    blk = er(nc.Block())
    def play(name):
        def runner(eng):
            for fn in ops[name]:
                fn(eng)
        return runner
    blk.sync(play("sync"))
    blk.tensor(play("tensor"))
    blk.vector(play("vector"))
    blk.scalar(play("scalar"))
    blk.gpsimd(play("gpsimd"))

    root.close()
    return nc


def _prep_inputs(inputs):
    f32 = np.float32
    x = np.ascontiguousarray(inputs["L0_attn_x"].reshape(TOKS, DM).astype(f32, copy=False))
    A_log = np.asarray(inputs["L0_attn_A_log"], dtype=f32)
    Dv = np.asarray(inputs["L0_attn_D"], dtype=f32)
    xpw = np.asarray(inputs["L0_attn_x_proj_w"], dtype=f32)
    dtw = np.asarray(inputs["L0_attn_dt_proj_w"], dtype=f32)
    dtb = np.asarray(inputs["L0_attn_dt_proj_b"], dtype=f32)
    gw = np.asarray(inputs["L0_ffn_gate_w"], dtype=np.int32)
    uw = np.asarray(inputs["L0_ffn_up_w"], dtype=np.int32)
    dw = np.asarray(inputs["L0_ffn_down_w"], dtype=np.int32)
    gs = np.asarray(inputs["L0_ffn_gate_s"], dtype=f32)
    us = np.asarray(inputs["L0_ffn_up_s"], dtype=f32)
    ds = np.asarray(inputs["L0_ffn_down_s"], dtype=f32)

    gwT = np.ascontiguousarray(gw.T)
    uwT = np.ascontiguousarray(uw.T)
    dwT = np.ascontiguousarray(dw.T)
    xpwT = np.ascontiguousarray(xpw.T)
    xpw_h = np.ascontiguousarray(
        xpwT.reshape(32, 128, RXP).transpose(1, 0, 2).reshape(128, 32 * RXP))
    dtwT = np.ascontiguousarray(dtw.T)

    ident = np.eye(128, dtype=f32)
    colsum = np.zeros((128, 128), dtype=f32); colsum[:, 0] = 1.0
    ones0 = np.zeros((128, 128), dtype=f32); ones0[0, :] = 1.0
    sel = np.zeros((32, 32 * 128), dtype=ml_dtypes.bfloat16)
    for r in range(32):
        sel[r, r * 128:(r + 1) * 128] = 1.0

    in_maps = []
    for c in range(NC):
        d0 = c * DSL
        xf = []
        for phase in range(2):
            gA = (c // 4) * SEQ + phase * 2048 + (c % 4) * TPH
            xf.append(x[gA:gA + TPH])
        x_ffn = np.ascontiguousarray(np.concatenate(xf, axis=0))
        A_slice = A_log[d0:d0 + DSL].reshape(NCH, 128, NST).transpose(1, 0, 2).reshape(128, NCH * NST)
        im = dict(
            x_tok=np.ascontiguousarray(x[c * 1024:(c + 1) * 1024]),
            x_dsl=np.ascontiguousarray(x[:, d0:d0 + DSL]),
            x_ffn=x_ffn,
            xpw_h=xpw_h,
            dtw_c=np.ascontiguousarray(dtwT[:, d0:d0 + DSL]),
            A_c=np.ascontiguousarray(A_slice),
            D_c=np.ascontiguousarray(Dv[d0:d0 + DSL].reshape(NCH, 128).T),
            dtb_c=np.ascontiguousarray(dtb[d0:d0 + DSL].reshape(NCH, 128).T),
            sel_in=sel,
            consts_in=np.ascontiguousarray(
                np.broadcast_to(np.array([1.0, EPS, -1.0, -0.5], dtype=f32), (128, 4))),
            ident_in=ident,
            colsum_in=colsum,
            ones0_in=ones0,
            gs_in=np.ascontiguousarray(np.broadcast_to(gs.reshape(1, 1), (128, 1))).astype(f32),
            us_in=np.ascontiguousarray(np.broadcast_to(us.reshape(1, 1), (128, 1))).astype(f32),
            ds_in=np.ascontiguousarray(np.broadcast_to(ds.reshape(1, 1), (128, 1))).astype(f32),
        )
        if STOP == "full":
            im.update(gwT=gwT, uwT=uwT, dwT=dwT)
        in_maps.append(im)
    return in_maps


def kernel(**inputs):
    in_maps = _prep_inputs(inputs)
    with allow_ici_all_to_all():
        nc = build_program()
        res = run_bass_kernel_spmd(nc, in_maps, core_ids=list(range(NC)))
    out = np.empty((TOKS, DM), dtype=np.float32)
    for c in range(NC):
        oc = res.results[c]["out_c"]
        for phase in range(2):
            g = (c // 4) * SEQ + phase * 2048 + (c % 4) * TPH
            out[g:g + TPH] = oc[phase * TPH:(phase + 1) * TPH]
    return out.reshape(BB, SEQ, DM)
